# version 02
# baseline (speedup 1.0000x reference)
import numpy as np
import concourse.bass as bass
import concourse.mybir as mybir
from concourse.bass_utils import run_bass_kernel_spmd
from contextlib import ExitStack

F32 = mybir.dt.float32
BF16 = mybir.dt.bfloat16
U32 = mybir.dt.uint32
ALU = mybir.AluOpType
AF = mybir.ActivationFunctionType
AXX = mybir.AxisListType.X

D = 1024
HA = 16
DK = 64
HF = 8
HD = 128
SK = 512
KT = 16
NE = 16
FP = 256
MAXREL = 128


from contextlib import contextmanager


class Sem:
    __slots__ = ("h", "_count")

    def __init__(self, h):
        self.h = h
        self._count = 0


class _RecInst:
    __slots__ = ("incs",)

    def __init__(self):
        self.incs = []

    def then_inc(self, semh, k):
        self.incs.append((semh, k))
        return self


class _Rec:

    def __init__(self):
        self._ops = []

    def __getattr__(self, name):
        def call(*a, **kw):
            ph = _RecInst()
            self._ops.append((name, a, kw, ph))
            return ph
        return call

    def _replay(self, eng):
        for name, a, kw, ph in self._ops:
            inst = getattr(eng, name)(*a, **kw)
            for semh, k in ph.incs:
                inst.then_inc(semh, k)


class P:

    def __init__(self, nc, es, blk):
        self.nc, self.es, self.blk = nc, es, blk
        self._sems = {}

    def sem(self, name):
        if name not in self._sems:
            h = self.es.enter_context(self.nc.semaphore(name))
            self._sems[name] = Sem(h)
        return self._sems[name]

    @contextmanager
    def use(self, engine):
        rec = _Rec()
        yield rec
        if rec._ops:
            getattr(self.blk, engine)(rec._replay)

    @staticmethod
    def inc(inst, sem, k=1):
        inst.then_inc(sem.h, k)
        sem._count += k
        return sem._count

    @staticmethod
    def wait(eng, sem, n=None):
        eng.wait_ge(sem.h, sem._count if n is None else n)


def sbuf(p, name, shape, dt=F32, es=None):
    es = es if es is not None else p.es
    return es.enter_context(p.nc.sbuf_tensor(name, shape, dt))


def psum(p, name, shape, dt=F32, es=None):
    es = es if es is not None else p.es
    per = {F32: 512, BF16: 1024}[dt]
    shape = list(shape)
    if len(shape) == 2 and shape[1] % per != 0:
        pad = ((shape[1] + per - 1) // per) * per
        t = es.enter_context(p.nc.psum_tensor(name, [shape[0], pad], dt))
        return t[:, 0:shape[1]]
    return es.enter_context(p.nc.psum_tensor(name, shape, dt))


def barrier(p, tag):
    s = p.sem("bar_" + tag)
    engs = ["sync", "vector", "scalar", "tensor", "gpsimd"]
    for e in engs:
        with p.use(e) as eng:
            P.inc(eng.drain(), s, 1)
    tgt = s._count
    for e in engs:
        with p.use(e) as eng:
            P.wait(eng, s, tgt)


def emit_ln_xnT(p, S, x_dram, xnTd_dram, xnTbf, mu_all, rstd_all, idf_sb, tag):
    NT = S // 128
    s_x = p.sem(tag + "_x")
    s_v = p.sem(tag + "_v")
    s_a = p.sem(tag + "_a")
    s_t = p.sem(tag + "_t")
    s_e = p.sem(tag + "_e")
    s_d = p.sem(tag + "_d")
    pes = ExitStack()
    xb = [sbuf(p, tag + "_xb%d" % i, [128, D], es=pes) for i in range(2)]
    sq = sbuf(p, tag + "_sq", [128, D], es=pes)
    st1 = sbuf(p, tag + "_st1", [128, NT], es=pes)
    vp = sbuf(p, tag + "_vp", [128, NT], es=pes)
    nb = sbuf(p, tag + "_nb", [128, NT], es=pes)
    xnf = [sbuf(p, tag + "_xnf%d" % i, [128, D], es=pes) for i in range(2)]
    stg = [sbuf(p, tag + "_stg%d" % i, [128, 8 * 128], es=pes) for i in range(2)]
    tps = [psum(p, tag + "_tp%d" % i, [128, 128], es=pes) for i in range(2)]
    a_done = []
    t_done = []

    for i in range(NT):
        w = i % 2
        with p.use("sync") as sy:
            if i >= 2:
                P.wait(sy, s_a, a_done[i - 2])
            P.inc(sy.dma_start(out=xb[w][:], in_=x_dram[i * 128:(i + 1) * 128, :]), s_x, 16)
        with p.use("scalar") as a:
            P.wait(a, s_x)
            P.inc(a.activation(out=sq[:], in_=xb[w][:], func=AF.Square,
                               accum_out=vp[:, i:i + 1]), s_a, 1)
        with p.use("vector") as v:
            P.wait(v, s_x)
            P.inc(v.reduce_sum(out=st1[:, i:i + 1], in_=xb[w][:], axis=AXX), s_v, 1)
            P.wait(v, s_v)
            P.inc(v.tensor_scalar(out=nb[:, i:i + 1], in0=st1[:, i:i + 1],
                                  scalar1=1.0 / D, scalar2=None, op0=ALU.mult), s_v, 1)
            P.wait(v, s_v)
            P.wait(v, s_a)
            P.inc(v.scalar_tensor_tensor(out=sq[:, 0:1], in0=nb[:, i:i + 1], scalar=-1.0,
                                         in1=nb[:, i:i + 1], op0=ALU.mult, op1=ALU.mult), s_v, 1)
            P.wait(v, s_v)
            P.inc(v.scalar_tensor_tensor(out=vp[:, i:i + 1], in0=vp[:, i:i + 1], scalar=1.0 / D,
                                         in1=sq[:, 0:1], op0=ALU.mult, op1=ALU.add), s_v, 1)
            P.wait(v, s_v)
            P.inc(v.tensor_scalar(out=vp[:, i:i + 1], in0=vp[:, i:i + 1], scalar1=1e-5,
                                  scalar2=None, op0=ALU.add), s_v, 1)
        with p.use("scalar") as a:
            P.wait(a, s_v)
            P.inc(a.activation(out=rstd_all[:, i:i + 1], in_=vp[:, i:i + 1], func=AF.Ln), s_a, 1)
            P.wait(a, s_a)
            P.inc(a.activation(out=rstd_all[:, i:i + 1], in_=rstd_all[:, i:i + 1],
                               func=AF.Exp, scale=-0.5), s_a, 1)
            P.wait(a, s_a)
        with p.use("vector") as v:
            P.wait(v, s_a)
            P.inc(v.tensor_copy(out=mu_all[:, i:i + 1], in_=nb[:, i:i + 1]), s_v, 1)
            P.wait(v, s_v)
            P.inc(v.scalar_tensor_tensor(out=nb[:, i:i + 1], in0=nb[:, i:i + 1], scalar=-1.0,
                                         in1=rstd_all[:, i:i + 1], op0=ALU.mult, op1=ALU.mult), s_v, 1)
        with p.use("scalar") as a:
            P.wait(a, s_v)
            if i >= 2:
                P.wait(a, s_t, t_done[i - 2])
            P.inc(a.activation(out=xnf[w][:], in_=xb[w][:], func=AF.Identity,
                               scale=rstd_all[:, i:i + 1], bias=nb[:, i:i + 1]), s_a, 1)
            a_done.append(s_a._count)
        for j in range(8):
            with p.use("tensor") as t:
                if j == 0:
                    P.wait(t, s_a)
                if i > 0 or j >= 2:
                    P.wait(t, s_e, s_e._count - 2)
                P.inc(t.transpose(out=tps[j % 2][:], in_=xnf[w][:, j * 128:(j + 1) * 128],
                                  identity=idf_sb[:]), s_t, 1)
            with p.use("vector") as v:
                P.wait(v, s_t)
                if j == 0 and i >= 2:
                    P.wait(v, s_d, 16 * (i - 1))
                P.inc(v.tensor_copy(out=stg[w][:, j * 128:(j + 1) * 128], in_=tps[j % 2][:]), s_e, 1)
                P.wait(v, s_e)
                P.inc(v.tensor_copy(out=xnTbf[j][:, i * 128:(i + 1) * 128], in_=tps[j % 2][:]), s_e, 1)
        t_done.append(s_t._count)
        with p.use("sync") as sy:
            P.wait(sy, s_e)
            P.inc(sy.dma_start(out=xnTd_dram.rearrange("(j p) s -> p j s", j=8)[:, :, i * 128:(i + 1) * 128],
                               in_=stg[w][:].rearrange("p (j c) -> p j c", j=8)), s_d, 16)
    with p.use("sync") as sy:
        P.wait(sy, s_d)
    pes.close()
    return s_e._count


def emit_cast_in(p, tag, dram_ap, rows, cols, out_sb, out_col0=0, dt=BF16, eng="sync"):
    s_l = p.sem("wc_cl")
    s_c = p.sem("wc_cc")
    if not hasattr(p, "_cast_buf"):
        p._cast_buf = [sbuf(p, "castbuf%d" % i, [128, 1024]) for i in range(2)]
        p._cast_w = 0
    w = p._cast_w
    p._cast_w = 1 - w
    buf = p._cast_buf[w]
    assert cols <= 1024 and rows <= 128
    with p.use(eng) as sy:
        if s_c._count >= 2:
            P.wait(sy, s_c, s_c._count - 1)
        P.inc(sy.dma_start(out=buf[0:rows, 0:cols], in_=dram_ap), s_l, 16)
    with p.use("scalar") as a:
        P.wait(a, s_l)
        P.inc(a.activation(out=out_sb[0:rows, out_col0:out_col0 + cols],
                           in_=buf[0:rows, 0:cols], func=AF.Identity), s_c, 1)
    return s_c._count


def emit_ffn_units(p, S, xnTd_dram, winf, kaT_d, kbT_d, uf, ev_dram, iota16, gall, merged,
                   dbg=None, do_gather=True):
    NT = S // 128
    NU = 8 * NT
    qs = list(range(0, S, 512))
    s_xl = p.sem("fu_xl")
    s_hmm = p.sem("fu_hmm")
    s_hev = p.sem("fu_hev")
    s_kl = p.sem("fu_kl")
    s_pe = p.sem("fu_pe")
    s_ev = p.sem("fu_ev")
    s_v = p.sem("fu_v")
    s_tk = p.sem("fu_tk")
    s_g = p.sem("fu_g")
    s_wv = p.sem("fu_wv")
    pes = ExitStack()
    xch = [sbuf(p, "fu_xch%d" % c, [128, 512], es=pes) for c in range(8)]
    hTf = [sbuf(p, "fu_hT%d" % i, [128, 512], es=pes) for i in range(2)]
    kab = [sbuf(p, "fu_ka%d" % i, [128, 512], es=pes) for i in range(2)]
    kbb = [sbuf(p, "fu_kb%d" % i, [128, 512], es=pes) for i in range(2)]
    ps_h = [psum(p, "fu_psh%d" % i, [128, 512], es=pes) for i in range(2)]
    ps_a = [psum(p, "fu_psa%d" % i, [128, 512], es=pes) for i in range(2)]
    ps_b = [psum(p, "fu_psb%d" % i, [128, 512], es=pes) for i in range(2)]
    ps_u = psum(p, "fu_psu", [128, 512], es=pes)
    gbufs = [sbuf(p, "fu_g%d" % i, [128, 2048], BF16, es=pes) for i in range(2)]
    widx = [sbuf(p, "fu_widx%d" % i, [128, 16], U32, es=pes) for i in range(2)]
    sa = [sbuf(p, "fu_sa%d" % i, [128, 512], es=pes) for i in range(2)]
    sb_ = [sbuf(p, "fu_sb%d" % i, [128, 512], es=pes) for i in range(2)]
    mr_ = sbuf(p, "fu_mr", [128, 512], es=pes)
    av = sbuf(p, "fu_av", [128, 16], es=pes)
    bv = sbuf(p, "fu_bv", [128, 16], es=pes)
    ai = sbuf(p, "fu_ai", [128, 16], U32, es=pes)
    bi = sbuf(p, "fu_bi", [128, 16], U32, es=pes)
    aif = sbuf(p, "fu_aif", [128, 16], es=pes)
    bif = sbuf(p, "fu_bif", [128, 16], es=pes)
    ps_ = sbuf(p, "fu_prods", [128, 256], es=pes)
    ps2_ = sbuf(p, "fu_prods2", [128, 256], es=pes)
    pv = sbuf(p, "fu_pv", [128, 16], es=pes)
    pos = sbuf(p, "fu_pos", [128, 16], U32, es=pes)
    rab = sbuf(p, "fu_rab", [128, 32], U32, es=pes)
    rabf = sbuf(p, "fu_rabf", [128, 32], es=pes)
    eqs = sbuf(p, "fu_eqs", [128, 256], es=pes)
    msc = sbuf(p, "fu_msc", [128, 256], es=pes)
    sel = sbuf(p, "fu_sel", [128, 32], es=pes)
    idxf = sbuf(p, "fu_idxf", [128, 16], es=pes)
    e16 = sbuf(p, "fu_e16", [128, 16], es=pes)
    zz = sbuf(p, "fu_zz", [128, 2], es=pes)
    w16 = sbuf(p, "fu_w16", [128, 16], es=pes)
    w16b = sbuf(p, "fu_w16b", [128, 16], BF16, es=pes)
    gwk = sbuf(p, "fu_gwk", [128, 2048], es=pes)
    tr1 = sbuf(p, "fu_tr1", [128, 1024], es=pes)
    tr2 = sbuf(p, "fu_tr2", [128, 512], es=pes)
    tr3 = sbuf(p, "fu_tr3", [128, 256], es=pes)

    def chain(v, inst):
        P.inc(inst, s_v, 1)
        P.wait(v, s_v)

    def topk_side(v, src, vals, idx):
        chain(v, v.max(out=vals[:, 0:8], in_=src[:]))
        chain(v, v.match_replace(out=mr_[:], in_to_replace=vals[:, 0:8], in_values=src[:],
                                 imm_value=-1e30))
        chain(v, v.max(out=vals[:, 8:16], in_=mr_[:]))
        chain(v, v.max_index(out=idx[:, 0:8], in_max=vals[:, 0:8], in_values=src[:]))
        chain(v, v.max_index(out=idx[:, 8:16], in_max=vals[:, 8:16], in_values=mr_[:]))

    u = -1
    kab_free = []
    for tq, q0 in enumerate(qs):
        cw = min(512, S - q0)
        ntl = cw // 128
        with p.use("sync") as sy:
            if tq > 0:
                P.wait(sy, s_hmm)
            P.inc(sy.dma_start(
                out=xch[0][:].tensor[:, 0:cw] if False else xch[0][0:128, 0:cw],
                in_=xnTd_dram[0:128, q0:q0 + cw]), s_xl, 16)
            for c in range(1, 8):
                P.inc(sy.dma_start(out=xch[c][0:128, 0:cw],
                                   in_=xnTd_dram[c * 128:(c + 1) * 128, q0:q0 + cw]), s_xl, 16)
        for h in range(8):
            hw = h % 2
            gh = tq * 8 + h
            with p.use("sync") as sy:
                if gh >= 2:
                    P.wait(sy, s_pe, kab_free[gh - 2])
                P.inc(sy.dma_start(out=kab[hw][:], in_=kaT_d[h]), s_kl, 16)
                P.inc(sy.dma_start(out=kbb[hw][:], in_=kbb_src(kbT_d, h)), s_kl, 16)
            pw = (tq * 8 + h) % 2
            with p.use("tensor") as t:
                P.wait(t, s_xl)
                if tq * 8 + h >= 2:
                    P.wait(t, s_hev, s_hev._count - 1)
                for c in range(8):
                    P.inc(t.matmul(out=ps_h[pw][:, 0:cw],
                                   lhsT=winf[:, c * 1024 + h * 128: c * 1024 + (h + 1) * 128],
                                   rhs=xch[c][0:128, 0:cw], start=(c == 0), stop=(c == 7)),
                          s_hmm, 1)
            with p.use("scalar") as a:
                P.wait(a, s_hmm)
                if tq * 8 + h >= 2:
                    P.wait(a, s_pe)
                P.inc(a.activation(out=hTf[hw][0:128, 0:cw], in_=ps_h[pw][0:128, 0:cw],
                                   func=AF.Identity), s_hev, 1)
            hev_now = s_hev._count
            for tl in range(ntl):
                u += 1
                ti = (q0 // 128) + tl
                w = u % 2
                with p.use("tensor") as t:
                    P.wait(t, s_ev)
                    if tl == 0:
                        P.wait(t, s_hev, hev_now)
                        P.wait(t, s_kl)
                    P.inc(t.matmul(out=ps_a[w][:], lhsT=hTf[hw][:, tl * 128:(tl + 1) * 128],
                                   rhs=kab[hw][:], start=True, stop=True), s_pe, 1)
                    P.inc(t.matmul(out=ps_b[w][:], lhsT=hTf[hw][:, tl * 128:(tl + 1) * 128],
                                   rhs=kbb[hw][:], start=True, stop=True), s_pe, 1)
                    P.inc(t.matmul(out=ps_u[:, 0:1], lhsT=hTf[hw][:, tl * 128:(tl + 1) * 128],
                                   rhs=uf[:, 0:1], start=True, stop=True), s_pe, 1)
                with p.use("scalar") as a:
                    P.wait(a, s_pe)
                    if u >= 2:
                        P.wait(a, s_tk, u - 1)
                    P.inc(a.activation(out=sa[w][:], in_=ps_a[w][:], func=AF.Identity), s_ev, 1)
                    P.inc(a.activation(out=sb_[w][:], in_=ps_b[w][:], func=AF.Identity), s_ev, 1)
                    P.inc(a.activation(out=gall[:, h * NT + ti:h * NT + ti + 1],
                                       in_=ps_u[:, 0:1], func=AF.Identity), s_ev, 1)
                with p.use("vector") as v:
                    P.wait(v, s_ev)
                    topk_side(v, sa[w], av, ai)
                    topk_side(v, sb_[w], bv, bi)
                    chain(v, v.tensor_copy(out=aif[:], in_=ai[:]))
                    chain(v, v.tensor_copy(out=bif[:], in_=bi[:]))
                    chain(v, v.scalar_tensor_tensor(
                        out=ps_[:], in0=av[:].rearrange("p (i j) -> p i j", j=1).to_broadcast([128, 16, 16]),
                        scalar=1.0, in1=bv[:].rearrange("p (i j) -> p i j", i=1).to_broadcast([128, 16, 16]),
                        op0=ALU.mult, op1=ALU.add))
                    chain(v, v.max(out=pv[:, 0:8], in_=ps_[:]))
                    chain(v, v.match_replace(out=ps2_[:], in_to_replace=pv[:, 0:8], in_values=ps_[:],
                                             imm_value=-1e30))
                    chain(v, v.max(out=pv[:, 8:16], in_=ps2_[:]))
                    chain(v, v.max_index(out=pos[:, 0:8], in_max=pv[:, 0:8], in_values=ps_[:]))
                    chain(v, v.max_index(out=pos[:, 8:16], in_max=pv[:, 8:16], in_values=ps2_[:]))
                    chain(v, v.tensor_scalar(out=rab[:, 0:16], in0=pos[:], scalar1=4, scalar2=None,
                                             op0=ALU.logical_shift_right))
                    chain(v, v.tensor_scalar(out=rab[:, 16:32], in0=pos[:], scalar1=15, scalar2=None,
                                             op0=ALU.bitwise_and))
                    chain(v, v.tensor_copy(out=rabf[:], in_=rab[:]))
                    for half, srcf in enumerate([aif, bif]):
                        rk = rabf[:, half * 16:(half + 1) * 16]
                        chain(v, v.tensor_tensor(
                            out=eqs[:], in0=rk.rearrange("p (k j) -> p k j", j=1).to_broadcast([128, 16, 16]),
                            in1=iota16[:].rearrange("p (k m) -> p k m", k=1).to_broadcast([128, 16, 16]),
                            op=ALU.is_equal))
                        chain(v, v.tensor_tensor(
                            out=msc[:], in0=eqs[:],
                            in1=srcf[:].rearrange("p (k m) -> p k m", k=1).to_broadcast([128, 16, 16]),
                            op=ALU.mult))
                        chain(v, v.reduce_sum(out=sel[:, half * 16:(half + 1) * 16],
                                              in_=msc[:].rearrange("p (k m) -> p k m", k=16), axis=AXX))
                    chain(v, v.scalar_tensor_tensor(out=idxf[:], in0=sel[:, 0:16], scalar=512.0,
                                                    in1=sel[:, 16:32], op0=ALU.mult, op1=ALU.add))
                    if dbg is not None:
                        chain(v, v.tensor_copy(out=dbg["idx"][:, u * 16:(u + 1) * 16], in_=idxf[:]))
                        chain(v, v.tensor_copy(out=dbg["pv"][:, u * 16:(u + 1) * 16], in_=pv[:]))
                    if u >= 2:
                        P.wait(v, s_g, 256 * (u - 1))
                    P.inc(v.tensor_copy(out=widx[w][:], in_=idxf[:]), s_tk, 1)
                    chain(v, v.tensor_scalar(out=e16[:], in0=pv[:], scalar1=pv[:, 0:1], scalar2=None,
                                             op0=ALU.subtract))
                with p.use("scalar") as a:
                    P.wait(a, s_v)
                    P.inc(a.activation(out=e16[:], in_=e16[:], func=AF.Exp), s_ev, 1)
                with p.use("vector") as v:
                    P.wait(v, s_ev)
                    chain(v, v.reduce_sum(out=zz[:, 0:1], in_=e16[:], axis=AXX))
                    chain(v, v.reciprocal(out=zz[:, 1:2], in_=zz[:, 0:1]))
                    chain(v, v.tensor_scalar(out=w16[:], in0=e16[:], scalar1=zz[:, 1:2], scalar2=None,
                                             op0=ALU.mult))
                    chain(v, v.tensor_copy(out=w16b[:], in_=w16[:]))
                if not do_gather:
                    with p.use("vector") as v:
                        P.wait(v, s_tk, u + 1)
                        chain(v, v.memset(merged[ti][:, h * 128:(h + 1) * 128], 0.0))
                        P.inc(v.nop(), s_g, 256)
                        P.inc(v.nop(), s_wv, 1)
                    continue
                with p.use("gpsimd") as g:
                    P.wait(g, s_tk, u + 1)
                    if u >= 2:
                        P.wait(g, s_wv, u - 1)
                    for k in range(16):
                        P.inc(g.indirect_dma_start(
                            out=gbufs[w][:, k * 128:(k + 1) * 128],
                            out_offset=None, in_=ev_dram[:, :],
                            in_offset=bass.IndirectOffsetOnAxis(ap=widx[w][:, k:k + 1], axis=0),
                        ), s_g, 16)
                with p.use("vector") as v:
                    P.wait(v, s_g, 256 * (u + 1))
                    chain(v, v.tensor_tensor(
                        out=gwk[:], in0=gbufs[w][:],
                        in1=w16b[:].rearrange("p (k d) -> p k d", d=1).to_broadcast([128, 16, 128]),
                        op=ALU.mult))
                    chain(v, v.tensor_tensor(out=tr1[:], in0=gwk[:, 0:1024], in1=gwk[:, 1024:2048], op=ALU.add))
                    chain(v, v.tensor_tensor(out=tr2[:], in0=tr1[:, 0:512], in1=tr1[:, 512:1024], op=ALU.add))
                    chain(v, v.tensor_tensor(out=tr3[:], in0=tr2[:, 0:256], in1=tr2[:, 256:512], op=ALU.add))
                    chain(v, v.tensor_tensor(out=merged[ti][:, h * 128:(h + 1) * 128],
                                             in0=tr3[:, 0:128], in1=tr3[:, 128:256], op=ALU.add))
                    P.inc(v.nop(), s_wv, 1)
            kab_free.append(s_pe._count)
    pes.close()
    return dict(s_g=s_g, s_tk=s_tk, s_v=s_v, s_wv=s_wv, NU=NU)


def kbb_src(kbT_d, h):
    return kbT_d[h]


def emit_replicate_row(p, tag, row_dram, n, out_sb, ones1=None):
    s_m = p.sem(tag + "_rm")
    with p.use("sync") as sy:
        P.inc(sy.dma_start(out=out_sb[:, 0:n], in_=row_dram.to_broadcast([128, n])), s_m, 16)
    return s_m


def emit_ffn_tail(p, S, gall, merged, woutT_d, lnw_d, lnb_d, g_dram, idbf_sb, s_wv):
    NT = S // 128
    s_a = p.sem("ft_a")
    s_v = p.sem("ft_v")
    s_t = p.sem("ft_t")
    s_e = p.sem("ft_e")
    s_mm = p.sem("ft_mm")
    s_gd = p.sem("ft_gd")
    pes = ExitStack()
    gates = sbuf(p, "ft_gates", [128, 8 * NT], es=pes)
    mT = [sbuf(p, "ft_mT%d" % j, [128, S], BF16, es=pes) for j in range(8)]
    woutbf = sbuf(p, "ft_woutbf", [128, 8 * 1024], BF16, es=pes)
    lnw_rep = sbuf(p, "ft_lnw", [128, D], es=pes)
    lnb_rep = sbuf(p, "ft_lnb", [128, D], es=pes)
    st12 = sbuf(p, "ft_st12", [128, 4], es=pes)
    sq = sbuf(p, "ft_sq", [128, 1024], es=pes)
    zn = [sbuf(p, "ft_zn%d" % i, [128, 1024], es=pes) for i in range(2)]
    tps = [psum(p, "ft_tp%d" % i, [128, 128], BF16, es=pes) for i in range(2)]
    pj2 = [psum(p, "ft_pj2_%d" % i, [128, 1024], es=pes) for i in range(2)]
    with p.use("sync") as sy:
        P.wait(sy, s_wv)
    with p.use("scalar") as a:
        P.wait(a, s_wv)
    for c in range(8):
        emit_cast_in(p, "wc", woutT_d[c * 128:(c + 1) * 128, :], 128, 1024, woutbf, c * 1024)
    emit_replicate_row(p, "ftlnw", lnw_d, D, lnw_rep)
    emit_replicate_row(p, "ftlnb", lnb_d, D, lnb_rep)
    with p.use("scalar") as a:
        P.wait(a, s_wv)
        P.wait(a, p.sem("ftlnw_rm"))
        P.wait(a, p.sem("ftlnb_rm"))
        P.wait(a, p.sem("wc_cc"))
        P.inc(a.activation(out=gates[:], in_=gall[:], func=AF.Sigmoid), s_a, 1)
    for ti in range(NT):
        with p.use("vector") as v:
            P.wait(v, s_a)
            for h in range(8):
                u = h * NT + ti
                P.inc(v.tensor_scalar(out=merged[ti][:, h * 128:(h + 1) * 128],
                                      in0=merged[ti][:, h * 128:(h + 1) * 128],
                                      scalar1=gates[:, u:u + 1], scalar2=None,
                                      op0=ALU.mult), s_v, 1)
    for ti in range(NT):
        for j in range(8):
            k = ti * 8 + j
            with p.use("tensor") as t:
                if k == 0:
                    P.wait(t, s_v)
                if k >= 2:
                    P.wait(t, s_e, s_e._count - 1)
                P.inc(t.transpose(out=tps[k % 2][:], in_=merged[ti][:, j * 128:(j + 1) * 128],
                                  identity=idbf_sb[:]), s_t, 1)
            with p.use("vector") as v:
                P.wait(v, s_t)
                P.inc(v.tensor_copy(out=mT[j][:, ti * 128:(ti + 1) * 128], in_=tps[k % 2][:]), s_e, 1)
    for ti in range(NT):
        w = ti % 2
        with p.use("tensor") as t:
            if ti == 0:
                P.wait(t, s_e)
            if ti >= 2:
                P.wait(t, s_a)
            for dh in range(2):
                for c in range(8):
                    P.inc(t.matmul(out=pj2[w][:, dh * 512:(dh + 1) * 512],
                                   lhsT=mT[c][:, ti * 128:(ti + 1) * 128],
                                   rhs=woutbf[:, c * 1024 + dh * 512: c * 1024 + dh * 512 + 512],
                                   start=(c == 0), stop=(c == 7)), s_mm, 1)
        with p.use("scalar") as a:
            P.wait(a, s_mm)
            P.inc(a.activation(out=sq[:], in_=pj2[w][:], func=AF.Square,
                               accum_out=st12[:, 1:2]), s_a, 1)
        with p.use("vector") as v:
            P.wait(v, s_mm)
            P.inc(v.reduce_sum(out=st12[:, 0:1], in_=pj2[w][:], axis=AXX), s_v, 1)
            P.wait(v, s_v)
            P.wait(v, s_a)
            P.inc(v.tensor_scalar(out=st12[:, 0:1], in0=st12[:, 0:1], scalar1=1.0 / D,
                                  scalar2=None, op0=ALU.mult), s_v, 1)
            P.wait(v, s_v)
            P.inc(v.scalar_tensor_tensor(out=st12[:, 2:3], in0=st12[:, 0:1], scalar=-1.0,
                                         in1=st12[:, 0:1], op0=ALU.mult, op1=ALU.mult), s_v, 1)
            P.wait(v, s_v)
            P.inc(v.scalar_tensor_tensor(out=st12[:, 1:2], in0=st12[:, 1:2], scalar=1.0 / D,
                                         in1=st12[:, 2:3], op0=ALU.mult, op1=ALU.add), s_v, 1)
            P.wait(v, s_v)
            P.inc(v.tensor_scalar(out=st12[:, 1:2], in0=st12[:, 1:2], scalar1=1e-5,
                                  scalar2=None, op0=ALU.add), s_v, 1)
        with p.use("scalar") as a:
            P.wait(a, s_v)
            P.inc(a.activation(out=st12[:, 1:2], in_=st12[:, 1:2], func=AF.Ln), s_a, 1)
            P.wait(a, s_a)
            P.inc(a.activation(out=st12[:, 1:2], in_=st12[:, 1:2], func=AF.Exp, scale=-0.5), s_a, 1)
            P.wait(a, s_a)
        with p.use("vector") as v:
            P.wait(v, s_a)
            P.inc(v.scalar_tensor_tensor(out=st12[:, 0:1], in0=st12[:, 0:1], scalar=-1.0,
                                         in1=st12[:, 1:2], op0=ALU.mult, op1=ALU.mult), s_v, 1)
        with p.use("scalar") as a:
            P.wait(a, s_v)
            if ti >= 2:
                P.wait(a, s_gd, 16 * (ti - 1))
            P.inc(a.activation(out=zn[w][:], in_=pj2[w][:], func=AF.Identity,
                               scale=st12[:, 1:2], bias=st12[:, 0:1]), s_a, 1)
        with p.use("vector") as v:
            P.wait(v, s_a)
            P.inc(v.tensor_tensor(out=zn[w][:], in0=zn[w][:], in1=lnw_rep[:], op=ALU.mult), s_v, 1)
            P.wait(v, s_v)
            P.inc(v.tensor_tensor(out=zn[w][:], in0=zn[w][:], in1=lnb_rep[:], op=ALU.add), s_v, 1)
            P.wait(v, s_v)
        with p.use("sync") as sy:
            P.wait(sy, s_v)
            P.inc(sy.dma_start(out=g_dram[ti * 128:(ti + 1) * 128, :], in_=zn[w][:]), s_gd, 16)
    s_sl = p.sem("ft_sl")
    B = s_gd._count
    for ti in range(NT):
        w = ti % 2
        with p.use("sync") as sy:
            P.wait(sy, s_gd, B + max(0, 32 * ti - 32))
            P.inc(sy.dma_start(out=zn[w][:], in_=g_dram[ti * 128:(ti + 1) * 128, :]), s_gd, 16)
        with p.use("scalar") as a:
            P.wait(a, s_gd)
            P.inc(a.activation(out=zn[w][:], in_=zn[w][:], func=AF.Silu), s_sl, 1)
        with p.use("sync") as sy:
            P.wait(sy, s_sl)
            P.inc(sy.dma_start(out=g_dram[ti * 128:(ti + 1) * 128, :], in_=zn[w][:]), s_gd, 16)
    with p.use("sync") as sy:
        P.wait(sy, s_gd)
    pes.close()
    return dict(s_v=s_v, s_a=s_a, s_gd=s_gd)


def emit_trevE(p, trev_dram, trevE_dram, ncol):
    s_l = p.sem("te_l")
    s_a = p.sem("te_a")
    s_o = p.sem("te_o")
    pes = ExitStack()
    CH = 1024
    nch = (ncol + CH - 1) // CH
    buf = [sbuf(p, "te_b%d" % i, [128, CH], es=pes) for i in range(2)]
    bufb = [sbuf(p, "te_c%d" % i, [128, CH], BF16, es=pes) for i in range(2)]
    k = 0
    for h in range(16):
        for ci in range(nch):
            c0 = ci * CH
            cl = min(CH, ncol - c0)
            w = k % 2
            with p.use("sync") as sy:
                if k >= 2:
                    P.wait(sy, s_a, k - 1)
                P.inc(sy.dma_start(out=buf[w][0:128, 0:cl], in_=trev_dram[h, :, c0:c0 + cl]), s_l, 16)
            with p.use("scalar") as a:
                P.wait(a, s_l)
                if k >= 2:
                    P.wait(a, s_o, 16 * (k - 1))
                P.inc(a.activation(out=bufb[w][0:128, 0:cl], in_=buf[w][0:128, 0:cl], func=AF.Exp), s_a, 1)
            with p.use("sync") as sy:
                P.wait(sy, s_a)
                P.inc(sy.dma_start(out=trevE_dram[h, :, c0:c0 + cl], in_=bufb[w][0:128, 0:cl]), s_o, 16)
            k += 1
    with p.use("sync") as sy:
        P.wait(sy, s_o)
    pes.close()
    return s_o


def emit_boprime(p, wo_d, bv_sb, bo_d, boprow):
    s_l = p.sem("bp_l")
    s_m = p.sem("bp_m")
    s_v = p.sem("bp_v")
    pes = ExitStack()
    wbuf = [sbuf(p, "bp_w%d" % i, [128, 1024], es=pes) for i in range(2)]
    bosb = sbuf(p, "bp_bo", [1, 1024], es=pes)
    psB = psum(p, "bp_ps", [128, 1024], es=pes)
    with p.use("sync") as sy:
        P.inc(sy.dma_start(out=bosb[:], in_=bo_d), s_l, 16)
    for c in range(8):
        w = c % 2
        with p.use("sync") as sy:
            if c >= 2:
                P.wait(sy, s_m, 2 * (c - 1))
            P.inc(sy.dma_start(out=wbuf[w][:], in_=wo_d[c * 128:(c + 1) * 128, :]), s_l, 16)
        with p.use("tensor") as t:
            P.wait(t, s_l)
            for dh in range(2):
                P.inc(t.matmul(out=psB[0:1, dh * 512:(dh + 1) * 512], lhsT=bv_sb[:, c:c + 1],
                               rhs=wbuf[w][:, dh * 512:(dh + 1) * 512],
                               start=(c == 0), stop=(c == 7)), s_m, 1)
    with p.use("vector") as v:
        P.wait(v, s_m)
        P.wait(v, s_l)
        P.inc(v.tensor_tensor(out=boprow[:], in0=psB[0:1, :], in1=bosb[:], op=ALU.add), s_v, 1)
    pes.close()
    return s_v


def emit_silu_batch(p, tiles, sem_wait, tag):
    s_a = p.sem(tag + "_silu")
    for i, tl in enumerate(tiles):
        with p.use("scalar") as a:
            if i == 0:
                P.wait(a, sem_wait)
            P.inc(a.activation(out=tl[:], in_=tl[:], func=AF.Silu), s_a, 1)
    return s_a


def emit_attention(p, S, tag, xnTbf, wq_d, wk_d, wv_d, wo_d, bq_sb, bk_sb, bv_sb,
                   g_dram, x_in_dram, x_out_dram, bo_dram, z_dram,
                   causal=False, trevE_dram=None, bconst_sb=None, CBAND=None, dbg=None):
    NT = S // 128
    NQC = (S + 511) // 512
    s_wl = p.sem(tag + "_wl")
    s_mm = p.sem(tag + "_mm")
    s_qk = p.sem(tag + "_qk")
    s_ex = p.sem(tag + "_ex")
    s_bd = p.sem(tag + "_bd")
    s_v = p.sem(tag + "_v")
    s_z = p.sem(tag + "_z")
    s_x = p.sem(tag + "_x")
    s_o = p.sem(tag + "_o")
    pes = ExitStack()
    wqp = sbuf(p, tag + "_wqp", [128, 1024], BF16, es=pes)
    wkp = sbuf(p, tag + "_wkp", [128, 1024], BF16, es=pes)
    wvp = sbuf(p, tag + "_wvp", [128, 1024], BF16, es=pes)
    wob = sbuf(p, tag + "_wob", [128, 8 * 1024], BF16, es=pes)
    QT = sbuf(p, tag + "_QT", [128, S], BF16, es=pes)
    KT = sbuf(p, tag + "_KT", [128, S], BF16, es=pes)
    V2 = [sbuf(p, tag + "_V2_%d" % i, [128, 130], BF16, es=pes) for i in range(NT)]
    ctxT = [sbuf(p, tag + "_ctx%d" % j, [128, S], BF16, es=pes) for j in range(8)]
    Pt = [sbuf(p, tag + "_P%d" % i, [128, 512], BF16, es=pes) for i in range(2)]
    band = [sbuf(p, tag + "_band%d" % i, [128, 512], BF16, es=pes) for i in range(2)]
    zbuf = sbuf(p, tag + "_zbuf", [1, 1024], es=pes)
    if causal:
        sqh = sbuf(p, tag + "_sqh", [128, 512], BF16, es=pes)
        drow = sbuf(p, tag + "_drow", [1, 2 * S], es=pes)
        km2 = sbuf(p, tag + "_km2", [1, 8], es=pes)
        crow = sbuf(p, tag + "_crow", [1, 2 * S], es=pes)
        negc = sbuf(p, tag + "_negc", [1, 2 * S], BF16, es=pes)
        ones64 = sbuf(p, tag + "_o64", [128, 1], BF16, es=pes)
        ones1b = sbuf(p, tag + "_o1b", [1, 128], BF16, es=pes)
    zrec = sbuf(p, tag + "_zrec", [1, 1024], es=pes)
    Rsb = sbuf(p, tag + "_Rsb", [128, 512], es=pes)
    ctmp = sbuf(p, tag + "_ctmp", [64, 512], es=pes)
    xb = [sbuf(p, tag + "_xb%d" % i, [128, 1024], es=pes) for i in range(2)]
    gb = [sbuf(p, tag + "_gb%d" % i, [128, 1024], es=pes) for i in range(2)]
    sfin = sbuf(p, tag + "_sfin", [128, 1024], es=pes)
    borep = sbuf(p, tag + "_borep", [128, 1024], es=pes)
    psQ = [psum(p, tag + "_psQ%d" % i, [128, 512], es=pes) for i in range(2)]
    psS = [psum(p, tag + "_psS%d" % i, [128, 512], es=pes) for i in range(2)]
    psC = [psum(p, tag + "_psC%d" % i, [128, 512], es=pes) for i in range(2)]

    with p.use("sync") as sy:
        P.inc(sy.dma_start(out=borep[:], in_=bo_dram.to_broadcast([128, 1024])), s_o, 16)
    with p.use("vector") as v:
        for i in range(NT):
            P.inc(v.memset(V2[i][:, 64:65], 1.0), s_v, 1)
            P.inc(v.memset(V2[i][:, 129:130], 1.0), s_v, 1)
        if causal:
            P.inc(v.memset(ones64[:], 1.0), s_v, 1)
            P.inc(v.memset(ones1b[:], 1.0), s_v, 1)

    for j in range(8):
        for c in range(8):
            emit_cast_in(p, "wc", wq_d[c * 128:(c + 1) * 128, j * 128:(j + 1) * 128], 128, 128, wqp, c * 128)
            emit_cast_in(p, "wc", wk_d[c * 128:(c + 1) * 128, j * 128:(j + 1) * 128], 128, 128, wkp, c * 128)
            emit_cast_in(p, "wc", wv_d[c * 128:(c + 1) * 128, j * 128:(j + 1) * 128], 128, 128, wvp, c * 128)
        with p.use("scalar") as a:
            P.inc(a.nop(), s_wl, 1)
        for which, (wp, out_sb, bias_sb) in enumerate([(wqp, QT, bq_sb), (wkp, KT, bk_sb)]):
            for qi in range(NQC):
                q0 = qi * 512
                cw = min(512, S - q0)
                pw = (which * NQC + qi) % 2
                with p.use("tensor") as t:
                    P.wait(t, s_wl)
                    P.wait(t, s_qk)
                    if j > 0:
                        P.wait(t, s_v)
                    for c in range(8):
                        P.inc(t.matmul(out=psQ[pw][:, 0:cw], lhsT=wp[:, c * 128:(c + 1) * 128],
                                       rhs=xnTbf[c][:, q0:q0 + cw], start=(c == 0), stop=(c == 7)), s_mm, 1)
                with p.use("scalar") as a:
                    P.wait(a, s_mm)
                    if bias_sb is not None:
                        P.inc(a.activation(out=out_sb[:, q0:q0 + cw], in_=psQ[pw][:, 0:cw],
                                           func=AF.Identity, bias=bias_sb[:, j:j + 1]), s_qk, 1)
                    else:
                        P.inc(a.activation(out=out_sb[:, q0:q0 + cw], in_=psQ[pw][:, 0:cw],
                                           func=AF.Identity), s_qk, 1)
        for ti in range(NT):
            pw = ti % 2
            with p.use("tensor") as t:
                P.wait(t, s_qk)
                for c in range(8):
                    P.inc(t.matmul(out=psQ[pw][:, 0:128], lhsT=xnTbf[c][:, ti * 128:(ti + 1) * 128],
                                   rhs=wvp[:, c * 128:(c + 1) * 128], start=(c == 0), stop=(c == 7)), s_mm, 1)
            with p.use("scalar") as a:
                P.wait(a, s_mm)
                P.inc(a.activation(out=V2[ti][:, 0:64], in_=psQ[pw][:, 0:64], func=AF.Identity), s_qk, 1)
                P.inc(a.activation(out=V2[ti][:, 65:129], in_=psQ[pw][:, 64:128], func=AF.Identity), s_qk, 1)
        qk_done = s_qk._count
        if causal:
            s_ns = p.sem(tag + "_ns")
            for qi in range(NQC):
                q0 = qi * 512
                cw = min(512, S - q0)
                for kind2, (srcA, srcB, row) in enumerate([(QT, QT, crow), (QT, KT, drow), (KT, KT, None)]):
                    with p.use("vector") as v:
                        P.wait(v, s_qk, qk_done)
                        P.wait(v, s_ns)
                        P.inc(v.tensor_tensor(out=sqh[:, 0:cw], in0=srcA[:, q0:q0 + cw],
                                              in1=srcB[:, q0:q0 + cw], op=ALU.mult), s_ns, 1)
                    with p.use("tensor") as t:
                        P.wait(t, s_ns)
                        P.inc(t.matmul(out=psC[0][0:1, 0:cw], lhsT=ones64[0:64, :], rhs=sqh[0:64, 0:cw],
                                       start=True, stop=True), s_ns, 1)
                        P.inc(t.matmul(out=psC[1][0:1, 0:cw], lhsT=ones64[64:128, :], rhs=sqh[64:128, 0:cw],
                                       start=True, stop=True), s_ns, 1)
                    if row is not None:
                        with p.use("scalar") as a:
                            P.wait(a, s_ns)
                            P.inc(a.activation(out=row[0:1, q0:q0 + cw], in_=psC[0][0:1, 0:cw],
                                               func=AF.Identity), s_ns, 1)
                            P.inc(a.activation(out=row[0:1, S + q0:S + q0 + cw], in_=psC[1][0:1, 0:cw],
                                               func=AF.Identity), s_ns, 1)
                            P.wait(a, s_ns)
                    else:
                        with p.use("scalar") as a:
                            P.wait(a, s_ns)
                            P.inc(a.activation(out=zbuf[0:1, 0:cw], in_=psC[0][0:1, 0:cw],
                                               func=AF.Identity), s_ns, 1)
                            P.inc(a.activation(out=zbuf[0:1, 512:512 + cw], in_=psC[1][0:1, 0:cw],
                                               func=AF.Identity), s_ns, 1)
                        with p.use("vector") as v:
                            P.wait(v, s_ns)
                            P.inc(v.reduce_max(out=km2[0:1, 4:5], in_=zbuf[0:1, 0:cw], axis=AXX), s_ns, 1)
                            P.wait(v, s_ns)
                            P.inc(v.reduce_max(out=km2[0:1, 5:6], in_=zbuf[0:1, 512:512 + cw], axis=AXX), s_ns, 1)
                            P.wait(v, s_ns)
                            if qi == 0:
                                P.inc(v.tensor_copy(out=km2[0:1, 0:2], in_=km2[0:1, 4:6]), s_ns, 1)
                            else:
                                P.inc(v.tensor_tensor(out=km2[0:1, 0:2], in0=km2[0:1, 0:2],
                                                      in1=km2[0:1, 4:6], op=ALU.max), s_ns, 1)
                            P.wait(v, s_ns)
            with p.use("scalar") as a:
                P.wait(a, s_ns)
                P.inc(a.activation(out=km2[0:1, 2:3], in_=km2[0:1, 0:1], func=AF.Ln), s_ns, 1)
                P.wait(a, s_ns)
                P.inc(a.activation(out=km2[0:1, 3:4], in_=km2[0:1, 1:2], func=AF.Ln), s_ns, 1)
                P.wait(a, s_ns)
                P.inc(a.activation(out=crow[0:1, 0:S], in_=crow[0:1, 0:S], func=AF.Ln), s_ns, 1)
                P.wait(a, s_ns)
                P.inc(a.activation(out=crow[0:1, S:2 * S], in_=crow[0:1, S:2 * S], func=AF.Ln), s_ns, 1)
                P.wait(a, s_ns)
            with p.use("vector") as v:
                P.wait(v, s_ns)
                P.inc(v.tensor_scalar(out=crow[0:1, 0:S], in0=crow[0:1, 0:S],
                                      scalar1=km2[0:1, 2:3], scalar2=None, op0=ALU.add), s_ns, 1)
                P.wait(v, s_ns)
                P.inc(v.tensor_scalar(out=crow[0:1, S:2 * S], in0=crow[0:1, S:2 * S],
                                      scalar1=km2[0:1, 3:4], scalar2=None, op0=ALU.add), s_ns, 1)
                P.wait(v, s_ns)
            with p.use("scalar") as a:
                P.wait(a, s_ns)
                P.inc(a.activation(out=crow[0:1, 0:2 * S], in_=crow[0:1, 0:2 * S],
                                   func=AF.Exp, scale=0.5), s_ns, 1)
                P.wait(a, s_ns)
            with p.use("vector") as v:
                P.wait(v, s_ns)
                P.inc(v.tensor_scalar(out=crow[0:1, 0:2 * S], in0=crow[0:1, 0:2 * S],
                                      scalar1=0.125, scalar2=-80.0, op0=ALU.mult, op1=ALU.add), s_ns, 1)
                P.wait(v, s_ns)
                P.inc(v.tensor_scalar(out=drow[0:1, 0:2 * S], in0=drow[0:1, 0:2 * S],
                                      scalar1=0.125, scalar2=None, op0=ALU.mult), s_ns, 1)
                P.wait(v, s_ns)
                P.inc(v.tensor_tensor(out=crow[0:1, 0:2 * S], in0=crow[0:1, 0:2 * S],
                                      in1=drow[0:1, 0:2 * S], op=ALU.max), s_ns, 1)
                P.wait(v, s_ns)
                P.inc(v.tensor_scalar(out=negc[0:1, 0:2 * S], in0=crow[0:1, 0:2 * S],
                                      scalar1=-8.0, scalar2=None, op0=ALU.mult), s_ns, 1)
                P.wait(v, s_ns)
        for qi in range(NQC):
            q0 = qi * 512
            cw = min(512, S - q0)
            nkt = NT if not causal else min(NT, (q0 + cw + 127) // 128)
            first = True
            for kt in range(nkt):
                Dq = q0 - 128 * kt
                if causal:
                    kind = "mask" if 128 * kt + 127 >= q0 else "plain"
                else:
                    kind = "hi" if Dq > 255 else ("lo" if Dq < -639 else "band")
                with p.use("tensor") as t:
                    if first:
                        P.wait(t, s_qk, qk_done)
                        if causal:
                            P.wait(t, p.sem(tag + "_ns"))
                    P.wait(t, s_ex)
                    P.inc(t.matmul(out=psS[0][:, 0:cw], lhsT=KT[0:64, kt * 128:(kt + 1) * 128],
                                   rhs=QT[0:64, q0:q0 + cw], start=True, stop=(not causal),
                                   tile_position=(0, 0)), s_mm, 1)
                    P.inc(t.matmul(out=psS[1][:, 0:cw], lhsT=KT[64:128, kt * 128:(kt + 1) * 128],
                                   rhs=QT[64:128, q0:q0 + cw], start=True, stop=(not causal),
                                   tile_position=(64, 0)), s_mm, 1)
                    if causal:
                        P.inc(t.matmul(out=psS[0][:, 0:cw], lhsT=ones1b[:],
                                       rhs=negc[0:1, q0:q0 + cw], start=False, stop=True), s_mm, 1)
                        P.inc(t.matmul(out=psS[1][:, 0:cw], lhsT=ones1b[:],
                                       rhs=negc[0:1, S + q0:S + q0 + cw], start=False, stop=True), s_mm, 1)
                if kind == "band":
                    with p.use("sync") as sy:
                        if s_bd._count >= 32:
                            P.wait(sy, s_v)
                        off = Dq + CBAND
                        P.inc(sy.dma_start(out=band[0][:], in_=trevE_dram[2 * j, :, off:off + 512]), s_bd, 16)
                        P.inc(sy.dma_start(out=band[1][:], in_=trevE_dram[2 * j + 1, :, off:off + 512]), s_bd, 16)
                with p.use("scalar") as a:
                    P.wait(a, s_mm)
                    for hh in range(2):
                        if causal:
                            P.inc(a.activation(out=Pt[hh][:, 0:cw], in_=psS[hh][:, 0:cw],
                                               func=AF.Exp, scale=0.125), s_ex, 1)
                        elif kind in ("band", "mask", "plain"):
                            P.inc(a.activation(out=Pt[hh][:, 0:cw], in_=psS[hh][:, 0:cw],
                                               func=AF.Exp, scale=0.125, bias=p._shiftcol[:, 0:1]), s_ex, 1)
                        else:
                            col = 2 * (2 * j + hh) + (1 if kind == "hi" else 0)
                            P.inc(a.activation(out=Pt[hh][:, 0:cw], in_=psS[hh][:, 0:cw],
                                               func=AF.Exp, scale=0.125,
                                               bias=bconst_sb[:, col:col + 1]), s_ex, 1)
                if kind in ("band", "mask"):
                    with p.use("vector") as v:
                        P.wait(v, s_ex)
                        if kind == "band":
                            P.wait(v, s_bd)
                            for hh in range(2):
                                P.inc(v.tensor_tensor(out=Pt[hh][:, 0:cw], in0=Pt[hh][:, 0:cw],
                                                      in1=band[hh][:, 0:cw], op=ALU.mult), s_v, 1)
                                P.wait(v, s_v)
                        else:
                            off = Dq + 384
                            for hh in range(2):
                                P.inc(v.tensor_tensor(out=Pt[hh][:, 0:cw], in0=Pt[hh][:, 0:cw],
                                                      in1=p._ctab[:, off:off + cw], op=ALU.mult), s_v, 1)
                                P.wait(v, s_v)
                with p.use("tensor") as t:
                    P.wait(t, s_v)
                    if kind not in ("band", "mask"):
                        P.wait(t, s_ex)
                    for hh in range(2):
                        P.inc(t.matmul(out=psC[hh][0:65, 0:cw], lhsT=V2[kt][:, hh * 65:(hh + 1) * 65],
                                       rhs=Pt[hh][:, 0:cw], start=first, stop=(kt == nkt - 1)), s_mm, 1)
                first = False
            with p.use("scalar") as a:
                P.wait(a, s_mm)
                P.inc(a.activation(out=zbuf[0:1, 0:cw], in_=psC[0][64:65, 0:cw], func=AF.Identity), s_z, 1)
                P.inc(a.activation(out=zbuf[0:1, 512:512 + cw], in_=psC[1][64:65, 0:cw], func=AF.Identity), s_z, 1)
            with p.use("vector") as v:
                P.wait(v, s_z)
                P.inc(v.reciprocal(out=zrec[0:1, 0:cw], in_=zbuf[0:1, 0:cw]), s_v, 1)
                P.wait(v, s_v)
                P.inc(v.reciprocal(out=zrec[0:1, 512:512 + cw], in_=zbuf[0:1, 512:512 + cw]), s_v, 1)
                P.wait(v, s_v)
                if getattr(p, "_zdbg", None) is not None and j == 0 and qi == 0:
                    P.inc(v.tensor_copy(out=p._zdbg[0:1, 0:1024], in_=zbuf[0:1, :]), s_v, 1)
                    P.wait(v, s_v)
                    P.inc(v.tensor_copy(out=p._zdbg[0:1, 1024:2048], in_=zrec[0:1, :]), s_v, 1)
                    P.wait(v, s_v)
            with p.use("sync") as sy:
                P.wait(sy, s_v)
                P.inc(sy.dma_start(out=z_dram[0:1, 0:1024], in_=zrec[0:1, 0:1024]), s_z, 16)
                P.wait(sy, s_z)
                P.inc(sy.dma_start(out=Rsb[0:64, 0:cw],
                                   in_=z_dram[0:1, 0:cw].to_broadcast([64, cw])), s_z, 16)
                P.inc(sy.dma_start(out=Rsb[64:128, 0:cw],
                                   in_=z_dram[0:1, 512:512 + cw].to_broadcast([64, cw])), s_z, 16)
            with p.use("vector") as v:
                P.wait(v, s_z)
                P.inc(v.tensor_tensor(out=ctmp[:, 0:cw], in0=psC[0][0:64, 0:cw],
                                      in1=Rsb[0:64, 0:cw], op=ALU.mult), s_v, 1)
                P.wait(v, s_v)
                P.inc(v.tensor_scalar(out=ctxT[j][0:64, q0:q0 + cw], in0=ctmp[0:64, 0:cw],
                                      scalar1=bv_sb[0:64, j:j + 1], scalar2=None, op0=ALU.add), s_v, 1)
                P.wait(v, s_v)
                P.inc(v.tensor_tensor(out=ctmp[:, 0:cw], in0=psC[1][0:64, 0:cw],
                                      in1=Rsb[64:128, 0:cw], op=ALU.mult), s_v, 1)
                P.wait(v, s_v)
                P.inc(v.tensor_scalar(out=ctxT[j][64:128, q0:q0 + cw], in0=ctmp[0:64, 0:cw],
                                      scalar1=bv_sb[64:128, j:j + 1], scalar2=None, op0=ALU.add), s_v, 1)
                P.wait(v, s_v)
    for c in range(8):
        emit_cast_in(p, "wc", wo_d[c * 128:(c + 1) * 128, :], 128, 1024, wob, c * 1024)
    ctx_done = s_v._count
    for ti in range(NT):
        w = ti % 2
        with p.use("sync") as sy:
            if ti >= 2:
                P.wait(sy, s_o, 16 + 16 * (ti - 1))
            P.inc(sy.dma_start(out=xb[w][:], in_=x_in_dram[ti * 128:(ti + 1) * 128, :]), s_x, 16)
            P.inc(sy.dma_start(out=gb[w][:], in_=g_dram[ti * 128:(ti + 1) * 128, :]), s_x, 16)
        with p.use("tensor") as t:
            if ti == 0:
                P.wait(t, s_v, ctx_done)
                P.wait(t, p.sem("wc_cc"))
            P.wait(t, s_v)
            for c in range(8):
                for dh in range(2):
                    P.inc(t.matmul(out=psQ[dh][:],
                                   lhsT=ctxT[c][:, ti * 128:(ti + 1) * 128],
                                   rhs=wob[:, c * 1024 + dh * 512: c * 1024 + dh * 512 + 512],
                                   start=(c == 0), stop=(c == 7)), s_mm, 1)
        with p.use("vector") as v:
            P.wait(v, s_mm)
            P.wait(v, s_x)
            P.inc(v.tensor_tensor(out=sfin[:, 0:512], in0=psQ[0][:], in1=borep[:, 0:512], op=ALU.add), s_v, 1)
            P.wait(v, s_v)
            P.inc(v.tensor_tensor(out=sfin[:, 512:1024], in0=psQ[1][:], in1=borep[:, 512:1024], op=ALU.add), s_v, 1)
            P.wait(v, s_v)
            P.inc(v.tensor_tensor(out=sfin[:], in0=sfin[:], in1=gb[w][:], op=ALU.mult), s_v, 1)
            P.wait(v, s_v)
            P.inc(v.tensor_tensor(out=xb[w][:], in0=xb[w][:], in1=sfin[:], op=ALU.add), s_v, 1)
        with p.use("sync") as sy:
            P.wait(sy, s_v)
            P.inc(sy.dma_start(out=x_out_dram[ti * 128:(ti + 1) * 128, :], in_=xb[w][:]), s_o, 16)
    with p.use("sync") as sy:
        P.wait(sy, s_o)
    pes.close()
    return dict(s_o=s_o)


def emit_l1_cast_w(p, W1cat_d, W2cat_d, W1bf_d, W2bf_d):
    s_l = p.sem("cw_l")
    s_a = p.sem("cw_a")
    s_o = p.sem("cw_o")
    pes = ExitStack()
    buf = [sbuf(p, "cw_b%d" % i, [128, 4096], es=pes) for i in range(2)]
    bufb = [sbuf(p, "cw_c%d" % i, [128, 4096], BF16, es=pes) for i in range(2)]
    jobs = [(W1cat_d[c * 128:(c + 1) * 128, :], W1bf_d[c * 128:(c + 1) * 128, :]) for c in range(8)]
    jobs.append((W1cat_d[1024:1025, :], W1bf_d[1024:1025, :]))
    for c in range(32):
        jobs.append((W2cat_d[c * 128:(c + 1) * 128, :], W2bf_d[c * 128:(c + 1) * 128, :]))
    for k, (src_ap, dst_ap) in enumerate(jobs):
        w = k % 2
        rows = src_ap.shape[0]
        cols = src_ap.shape[1]
        with p.use("sync") as sy:
            if k >= 2:
                P.wait(sy, s_a, k - 1)
            P.inc(sy.dma_start(out=buf[w][0:rows, 0:cols], in_=src_ap), s_l, 16)
        with p.use("scalar") as a:
            P.wait(a, s_l)
            if k >= 2:
                P.wait(a, s_o, 16 * (k - 1))
            P.inc(a.activation(out=bufb[w][0:rows, 0:cols], in_=buf[w][0:rows, 0:cols],
                               func=AF.Identity), s_a, 1)
        with p.use("sync") as sy:
            P.wait(sy, s_a)
            P.inc(sy.dma_start(out=dst_ap, in_=bufb[w][0:rows, 0:cols]), s_o, 16)
    with p.use("sync") as sy:
        P.wait(sy, s_o)
    pes.close()
    return s_o


def emit_l1_gate(p, S, xnTd_dram, wg_d, bg_d, selrow_d, selT_bf, idf_sb):
    NT = S // 128
    s_l = p.sem("lg_l")
    s_mm = p.sem("lg_mm")
    s_a = p.sem("lg_a")
    s_v = p.sem("lg_v")
    s_t = p.sem("lg_t")
    s_d = p.sem("lg_d")
    pes = ExitStack()
    xg = [sbuf(p, "lg_xg%d" % i, [128, 1024], es=pes) for i in range(2)]
    wgf = sbuf(p, "lg_wg", [128, 8 * 16], es=pes)
    bgr = sbuf(p, "lg_bg", [128, 16], es=pes)
    lg = sbuf(p, "lg_lg", [128, 16], es=pes)
    sc2 = sbuf(p, "lg_sc", [128, 20], es=pes)
    mx = sbuf(p, "lg_mx", [128, 8], es=pes)
    mx2 = sbuf(p, "lg_mx2", [128, 8], es=pes)
    scr2 = sbuf(p, "lg_scr2", [128, 16], es=pes)
    mtk = sbuf(p, "lg_mtk", [128, 16], es=pes)
    mTa = sbuf(p, "lg_mTa", [16, S], es=pes)
    zmask = sbuf(p, "lg_zm", [16, S], es=pes)
    incl = sbuf(p, "lg_incl", [16, S], es=pes)
    zer = sbuf(p, "lg_zer", [16, S], es=pes)
    thr = sbuf(p, "lg_thr", [16, 1], es=pes)
    selF = sbuf(p, "lg_selF", [16, S], es=pes)
    tpm = psum(p, "lg_tpm", [128, 128], es=pes)
    with p.use("sync") as sy:
        for c in range(8):
            P.inc(sy.dma_start(out=wgf[:, c * 16:(c + 1) * 16],
                               in_=wg_d[c * 128:(c + 1) * 128, :]), s_l, 16)
        P.inc(sy.dma_start(out=bgr[:], in_=bg_d.to_broadcast([128, 16])), s_l, 16)
    with p.use("vector") as v:
        P.inc(v.memset(mx2[:], -1e30), s_v, 1)
        P.inc(v.memset(zer[:], 0.0), s_v, 1)
    kload = s_l._count
    for ti in range(NT):
        w = ti % 2
        with p.use("sync") as sy:
            if ti >= 2:
                P.wait(sy, s_mm, 8 * (ti - 1))
            P.inc(sy.dma_start(out=xg[w][:].rearrange("p (c t) -> p c t", c=8),
                               in_=xnTd_dram.rearrange("(c p) s -> p c s", p=128)[:, :, ti * 128:(ti + 1) * 128]),
                  s_l, 16)
        with p.use("tensor") as t:
            P.wait(t, s_l)
            if ti >= 1:
                P.wait(t, s_a)
                P.wait(t, s_t)
            for c in range(8):
                P.inc(t.matmul(out=tpm[:, 0:16], lhsT=xg[w][:, c * 128:(c + 1) * 128],
                               rhs=wgf[:, c * 16:(c + 1) * 16], start=(c == 0), stop=(c == 7)), s_mm, 1)
        with p.use("scalar") as a:
            P.wait(a, s_mm)
            P.inc(a.activation(out=lg[:], in_=tpm[:, 0:16], func=AF.Identity), s_a, 1)
        with p.use("vector") as v:
            P.wait(v, s_a)
            P.inc(v.tensor_tensor(out=lg[:], in0=lg[:], in1=bgr[:], op=ALU.add), s_v, 1)
            P.wait(v, s_v)
            P.inc(v.reduce_max(out=sc2[:, 16:17], in_=lg[:], axis=AXX), s_v, 1)
            P.wait(v, s_v)
            P.inc(v.tensor_scalar(out=sc2[:, 0:16], in0=lg[:], scalar1=sc2[:, 16:17],
                                  scalar2=None, op0=ALU.subtract), s_v, 1)
            P.wait(v, s_v)
        with p.use("scalar") as a:
            P.wait(a, s_v)
            P.inc(a.activation(out=sc2[:, 0:16], in_=sc2[:, 0:16], func=AF.Exp), s_a, 1)
        with p.use("vector") as v:
            P.wait(v, s_a)
            P.inc(v.reduce_sum(out=sc2[:, 17:18], in_=sc2[:, 0:16], axis=AXX), s_v, 1)
            P.wait(v, s_v)
            P.inc(v.reciprocal(out=sc2[:, 18:19], in_=sc2[:, 17:18]), s_v, 1)
            P.wait(v, s_v)
            P.inc(v.tensor_scalar(out=sc2[:, 0:16], in0=sc2[:, 0:16], scalar1=sc2[:, 18:19],
                                  scalar2=None, op0=ALU.mult), s_v, 1)
            P.wait(v, s_v)
            P.inc(v.max(out=mx[:], in_=sc2[:, 0:16]), s_v, 1)
            P.wait(v, s_v)
            P.inc(v.tensor_copy(out=mx2[:, 0:2], in_=mx[:, 0:2]), s_v, 1)
            P.wait(v, s_v)
            P.inc(v.match_replace(out=scr2[:], in_to_replace=mx2[:], in_values=sc2[:, 0:16],
                                  imm_value=-1e30), s_v, 1)
            P.wait(v, s_v)
            P.inc(v.tensor_tensor(out=mtk[:], in0=scr2[:], in1=sc2[:, 0:16], op=ALU.not_equal), s_v, 1)
            P.wait(v, s_v)
        with p.use("tensor") as t:
            P.wait(t, s_v)
            P.inc(t.transpose(out=tpm[0:16, 0:128], in_=mtk[:], identity=idf_sb[:]), s_t, 1)
        with p.use("vector") as v:
            P.wait(v, s_t)
            P.inc(v.tensor_copy(out=mTa[0:16, ti * 128:(ti + 1) * 128], in_=tpm[0:16, 0:128]), s_t, 1)
    with p.use("vector") as v:
        P.wait(v, s_t)
        P.inc(v.tensor_scalar(out=zmask[:], in0=mTa[:], scalar1=-1.0, scalar2=1.0,
                              op0=ALU.mult, op1=ALU.add), s_v, 1)
        P.wait(v, s_v)
        P.inc(v.tensor_tensor_scan(out=incl[:], data0=zmask[:], data1=zer[:], initial=0.0,
                                   op0=ALU.add, op1=ALU.add), s_v, 1)
        P.wait(v, s_v)
        P.inc(v.tensor_scalar(out=thr[:], in0=incl[:, S - 1:S], scalar1=-float(S // 2),
                              scalar2=None, op0=ALU.add), s_v, 1)
        P.wait(v, s_v)
        P.inc(v.tensor_tensor(out=incl[:], in0=incl[:], in1=zmask[:], op=ALU.subtract), s_v, 1)
        P.wait(v, s_v)
        P.inc(v.tensor_scalar(out=incl[:], in0=incl[:], scalar1=thr[:], scalar2=None,
                              op0=ALU.is_lt), s_v, 1)
        P.wait(v, s_v)
        P.inc(v.tensor_tensor(out=incl[:], in0=incl[:], in1=zmask[:], op=ALU.mult), s_v, 1)
        P.wait(v, s_v)
        P.inc(v.tensor_tensor(out=selF[:], in0=mTa[:], in1=incl[:], op=ALU.max), s_v, 1)
        P.wait(v, s_v)
        P.inc(v.tensor_copy(out=selT_bf[:], in_=selF[:]), s_v, 1)
        P.wait(v, s_v)
    with p.use("sync") as sy:
        P.wait(sy, s_v)
        P.inc(sy.dma_start(out=selrow_d[:, :], in_=selF[:]), s_d, 16)
        P.wait(sy, s_d)
    pes.close()
    return dict(s_d=s_d, s_v=s_v)


def emit_l1_experts(p, S, xnTbf, W1bf_d, W2bf_d, b2_d, selrow_d, selT_bf, gam_d, bet_d,
                    mu_all, rstd_all, x_in_dram, g_dram, s_gate_d):
    NT = S // 128
    NG = (NT + 3) // 4
    s_wl = p.sem("lx_wl")
    s_mm = p.sem("lx_mm")
    s_a = p.sem("lx_a")
    s_v = p.sem("lx_v")
    s_mr = p.sem("lx_mr")
    s_x = p.sem("lx_x")
    s_gd = p.sem("lx_gd")
    pes = ExitStack()
    w1b = [sbuf(p, "lx_w1b%d" % i, [128, 1024], BF16, es=pes) for i in range(2)]
    b1r = [sbuf(p, "lx_b1r%d" % i, [1, 128], BF16, es=pes) for i in range(2)]
    onesr = sbuf(p, "lx_ones", [1, 512], BF16, es=pes)
    w2b = [sbuf(p, "lx_w2b%d" % i, [128, 1024], BF16, es=pes) for i in range(2)]
    b2b = sbuf(p, "lx_b2b", [16, 1024], BF16, es=pes)
    stash = [sbuf(p, "lx_st%d" % e, [128, 512], BF16, es=pes) for e in range(32)]
    mrep = [sbuf(p, "lx_mrep%d" % i, [128, 512], BF16, es=pes) for i in range(2)]
    gamr = sbuf(p, "lx_gam", [128, 1024], es=pes)
    betr = sbuf(p, "lx_bet", [128, 1024], es=pes)
    xb = [sbuf(p, "lx_xb%d" % i, [128, 1024], es=pes) for i in range(2)]
    zb = [sbuf(p, "lx_zb%d" % i, [128, 1024], es=pes) for i in range(2)]
    nbc = sbuf(p, "lx_nbc", [128, 4], es=pes)
    sq = sbuf(p, "lx_sq", [128, 1024], es=pes)
    psO = [psum(p, "lx_psO%d" % i, [128, 1024], es=pes) for i in range(4)]
    psH = [psO[0][:, 0:512], psO[1][:, 0:512]]
    emit_replicate_row(p, "lxgam", gam_d, D, gamr)
    emit_replicate_row(p, "lxbet", bet_d, D, betr)
    with p.use("vector") as v:
        P.inc(v.memset(onesr[:], 1.0), s_v, 1)
    with p.use("scalar") as a:
        pass
    b2f = sbuf(p, "lx_b2f", [16, 1024], es=pes)
    with p.use("sync") as sy:
        P.inc(sy.dma_start(out=b2f[:], in_=b2_d[:, :]), s_wl, 16)
    with p.use("scalar") as a:
        P.wait(a, s_wl)
        P.inc(a.activation(out=b2b[:], in_=b2f[:], func=AF.Identity), s_a, 1)
    for grp in range(NG):
        t0g = grp * 4
        ntl = min(4, NT - t0g)
        cw = 128 * ntl
        q0 = t0g * 128
        for ef in range(32):
            w = ef % 2
            e = ef // 2
            with p.use("sync") as sy:
                if grp > 0 or ef >= 2:
                    P.wait(sy, s_mm)
                for c in range(8):
                    P.inc(sy.dma_start(out=w1b[w][:, c * 128:(c + 1) * 128],
                                       in_=W1bf_d[c * 128:(c + 1) * 128, ef * 128:(ef + 1) * 128]), s_wl, 16)
                P.inc(sy.dma_start(out=b1r[w][:], in_=W1bf_d[1024:1025, ef * 128:(ef + 1) * 128]), s_wl, 16)

            if ef % 2 == 0:
                with p.use("gpsimd") as g:
                    P.wait(g, s_v)
                    P.inc(g.dma_start(out=mrep[(ef // 2) % 2][:, 0:cw],
                                      in_=selrow_d[e:e + 1, q0:q0 + cw].to_broadcast([128, cw])), s_wl, 16)
            with p.use("tensor") as t:
                P.wait(t, s_wl)
                if grp > 0 or ef >= 2:
                    P.wait(t, s_a)
                if grp > 0 and ef < 2:
                    P.wait(t, s_v)
                for c in range(8):
                    P.inc(t.matmul(out=psH[w][:, 0:cw], lhsT=w1b[w][:, c * 128:(c + 1) * 128],
                                   rhs=xnTbf[c][:, q0:q0 + cw], start=(c == 0), stop=False), s_mm, 1)
                P.inc(t.matmul(out=psH[w][:, 0:cw], lhsT=b1r[w][:], rhs=onesr[0:1, 0:cw],
                               start=False, stop=True), s_mm, 1)
            with p.use("scalar") as a:
                P.wait(a, s_mm)
                if grp > 0:
                    P.wait(a, s_mm)
                P.inc(a.activation(out=stash[ef][:, 0:cw], in_=psH[w][:, 0:cw], func=AF.Gelu), s_a, 1)
            with p.use("vector") as v:
                P.wait(v, s_a)
                P.wait(v, s_wl)
                P.inc(v.tensor_tensor(out=stash[ef][:, 0:cw], in0=stash[ef][:, 0:cw],
                                      in1=mrep[(ef // 2) % 2][:, 0:cw], op=ALU.mult), s_v, 1)
                P.wait(v, s_v)
        stash_done = s_v._count
        for ef in range(32):
            w = ef % 2
            with p.use("sync") as sy:
                P.wait(sy, s_mm)
                P.inc(sy.dma_start(out=w2b[w][:], in_=W2bf_d[ef * 128:(ef + 1) * 128, :]), s_wl, 16)
            with p.use("tensor") as t:
                P.wait(t, s_wl)
                if ef == 0:
                    P.wait(t, s_v, stash_done)
                for tl in range(ntl):
                    for dh in range(2):
                        P.inc(t.matmul(out=psO[tl][:, dh * 512:(dh + 1) * 512],
                                       lhsT=stash[ef][:, tl * 128:(tl + 1) * 128],
                                       rhs=w2b[w][:, dh * 512:(dh + 1) * 512],
                                       start=(ef == 0), stop=False), s_mm, 1)
        with p.use("tensor") as t:
            for tl in range(ntl):
                for dh in range(2):
                    P.inc(t.matmul(out=psO[tl][:, dh * 512:(dh + 1) * 512],
                                   lhsT=selT_bf[0:16, q0 + tl * 128:q0 + (tl + 1) * 128],
                                   rhs=b2b[0:16, dh * 512:(dh + 1) * 512],
                                   start=False, stop=True), s_mm, 1)
        for tl in range(ntl):
            ti = t0g + tl
            w = tl % 2
            with p.use("sync") as sy:
                if ti >= 2:
                    P.wait(sy, s_gd, 16 * (ti - 1))
                P.inc(sy.dma_start(out=xb[w][:], in_=x_in_dram[ti * 128:(ti + 1) * 128, :]), s_x, 16)
            with p.use("vector") as v:
                P.wait(v, s_x)
                P.inc(v.scalar_tensor_tensor(out=nbc[:, 0:1], in0=mu_all[:, ti:ti + 1], scalar=-1.0,
                                             in1=rstd_all[:, ti:ti + 1], op0=ALU.mult, op1=ALU.mult), s_v, 1)
                P.wait(v, s_v)
            with p.use("scalar") as a:
                P.wait(a, s_v)
                P.inc(a.activation(out=zb[w][:], in_=xb[w][:], func=AF.Identity,
                                   scale=rstd_all[:, ti:ti + 1], bias=nbc[:, 0:1]), s_a, 1)
            with p.use("vector") as v:
                P.wait(v, s_a)
                P.wait(v, s_mm)
                P.inc(v.tensor_tensor(out=zb[w][:], in0=zb[w][:], in1=psO[tl][:], op=ALU.add), s_v, 1)
                P.wait(v, s_v)
                P.inc(v.reduce_sum(out=nbc[:, 1:2], in_=zb[w][:], axis=AXX), s_v, 1)
                P.wait(v, s_v)
            with p.use("scalar") as a:
                P.wait(a, s_v)
                P.inc(a.activation(out=sq[:], in_=zb[w][:], func=AF.Square,
                                   accum_out=nbc[:, 2:3]), s_a, 1)
            with p.use("vector") as v:
                P.wait(v, s_a)
                P.inc(v.tensor_scalar(out=nbc[:, 1:2], in0=nbc[:, 1:2], scalar1=1.0 / D,
                                      scalar2=None, op0=ALU.mult), s_v, 1)
                P.wait(v, s_v)
                P.inc(v.scalar_tensor_tensor(out=nbc[:, 3:4], in0=nbc[:, 1:2], scalar=-1.0,
                                             in1=nbc[:, 1:2], op0=ALU.mult, op1=ALU.mult), s_v, 1)
                P.wait(v, s_v)
                P.inc(v.scalar_tensor_tensor(out=nbc[:, 2:3], in0=nbc[:, 2:3], scalar=1.0 / D,
                                             in1=nbc[:, 3:4], op0=ALU.mult, op1=ALU.add), s_v, 1)
                P.wait(v, s_v)
                P.inc(v.tensor_scalar(out=nbc[:, 2:3], in0=nbc[:, 2:3], scalar1=1e-5,
                                      scalar2=None, op0=ALU.add), s_v, 1)
            with p.use("scalar") as a:
                P.wait(a, s_v)
                P.inc(a.activation(out=nbc[:, 2:3], in_=nbc[:, 2:3], func=AF.Ln), s_a, 1)
                P.wait(a, s_a)
                P.inc(a.activation(out=nbc[:, 2:3], in_=nbc[:, 2:3], func=AF.Exp, scale=-0.5), s_a, 1)
                P.wait(a, s_a)
            with p.use("vector") as v:
                P.wait(v, s_a)
                P.inc(v.scalar_tensor_tensor(out=nbc[:, 1:2], in0=nbc[:, 1:2], scalar=-1.0,
                                             in1=nbc[:, 2:3], op0=ALU.mult, op1=ALU.mult), s_v, 1)
            with p.use("scalar") as a:
                P.wait(a, s_v)
                P.inc(a.activation(out=zb[w][:], in_=zb[w][:], func=AF.Identity,
                                   scale=nbc[:, 2:3], bias=nbc[:, 1:2]), s_a, 1)
            with p.use("vector") as v:
                P.wait(v, s_a)
                P.inc(v.tensor_tensor(out=zb[w][:], in0=zb[w][:], in1=gamr[:], op=ALU.mult), s_v, 1)
                P.wait(v, s_v)
                P.inc(v.tensor_tensor(out=zb[w][:], in0=zb[w][:], in1=betr[:], op=ALU.add), s_v, 1)
                P.wait(v, s_v)
            with p.use("sync") as sy:
                P.wait(sy, s_v)
                P.inc(sy.dma_start(out=g_dram[ti * 128:(ti + 1) * 128, :], in_=zb[w][:]), s_gd, 16)
    s_sl = p.sem("lx_sl")
    B = s_gd._count
    for ti in range(NT):
        w = ti % 2
        with p.use("sync") as sy:
            P.wait(sy, s_gd, B + max(0, 32 * ti - 32))
            P.inc(sy.dma_start(out=zb[w][:], in_=g_dram[ti * 128:(ti + 1) * 128, :]), s_gd, 16)
        with p.use("scalar") as a:
            P.wait(a, s_gd)
            P.inc(a.activation(out=zb[w][:], in_=zb[w][:], func=AF.Silu), s_sl, 1)
        with p.use("sync") as sy:
            P.wait(sy, s_sl)
            P.inc(sy.dma_start(out=g_dram[ti * 128:(ti + 1) * 128, :], in_=zb[w][:]), s_gd, 16)
    with p.use("sync") as sy:
        P.wait(sy, s_gd)
    pes.close()
    return dict(s_gd=s_gd)


CB_ = 2047
NCOL_ = 2 * CB_ + 513
SEQ = 2048
NROW_EV = 262144


def build_l0(S):
    nc = bass.Bass("TRN2", target_bir_lowering=False)
    dr = lambda n, s, dt=F32, k="ExternalInput": nc.dram_tensor(n, s, dt, kind=k).ap()
    x_d = dr("x0", [S, D])
    WinT_d = dr("WinT", [D, D]); kaT_d = dr("kaT", [8, 128, 512]); kbT_d = dr("kbT", [8, 128, 512])
    ush_d = dr("ush", [128, 1]); ev_d = dr("ev", [NROW_EV, 128])
    WoutT_d = dr("WoutT", [D, D]); lnw_d = dr("lnw", [1, D]); lnb_d = dr("lnb", [1, D])
    WqT_d = dr("WqT", [D, D]); WkT_d = dr("WkT", [D, D]); WvT_d = dr("WvT", [D, D]); WoT_d = dr("WoT", [D, D])
    bq_d = dr("bq", [128, 8]); bk_d = dr("bk", [128, 8]); bv_d = dr("bv", [128, 8]); bo_d = dr("bo", [1, D])
    trev_d = dr("trev", [16, 128, NCOL_]); rbfar_d = dr("rbfar", [1, 32])
    idbf_d = dr("idbf", [128, 128], BF16); idf_d = dr("idf", [128, 128])
    iota_d = dr("iota16", [128, 16])
    trevE_d = dr("trevE", [16, 128, NCOL_], BF16, k="Internal")
    z_d = dr("z_d", [1, D], F32, k="Internal")
    xnTd_d = dr("xnTd", [D, S], F32, k="Internal")
    g_d = dr("g_d", [S, D], F32, k="Internal")
    o_x1 = dr("x1", [S, D], F32, k="ExternalOutput")
    es = ExitStack()
    with es:
        blk = es.enter_context(nc.Block())
        p = P(nc, es, blk)
        emit_l0(p, S, locals())
    return nc


def emit_l0(p, S, dd):
    NT = S // 128
    x_d = dd["x_d"]
    p._cast_buf = [sbuf(p, "castbuf%d" % i, [128, 1024]) for i in range(2)]
    p._cast_w = 0
    xnTbf = [sbuf(p, "xnTb%d" % j, [128, S], BF16) for j in range(8)]
    idsb = sbuf(p, "idsb", [128, 128], BF16); idfsb = sbuf(p, "idfsb", [128, 128])
    iota16 = sbuf(p, "iota16s", [128, 16])
    uf = sbuf(p, "uf", [128, 1])
    bqs = sbuf(p, "bqs", [128, 8]); bks = sbuf(p, "bks", [128, 8]); bvs = sbuf(p, "bvs", [128, 8])
    bcsb = sbuf(p, "bcsb", [128, 32])
    shiftc = sbuf(p, "shiftc", [128, 1])
    p._shiftcol = shiftc
    mu_all = sbuf(p, "mu", [128, NT])
    rstd_all = sbuf(p, "rstd", [128, NT])
    es_a = ExitStack()
    winf = sbuf(p, "winf", [128, 8 * 1024], es=es_a)
    gall = sbuf(p, "gall", [128, 8 * NT], es=es_a)
    merged = [sbuf(p, "mg%d" % t, [128, D], BF16, es=es_a) for t in range(NT)]
    ld = p.sem("ld")
    with p.use("sync") as sy:
        for dst, srcd in [(idsb, dd["idbf_d"]), (idfsb, dd["idf_d"]), (iota16, dd["iota_d"]),
                          (uf, dd["ush_d"]), (bqs, dd["bq_d"]), (bks, dd["bk_d"]), (bvs, dd["bv_d"])]:
            P.inc(sy.dma_start(out=dst[:], in_=srcd[:]), ld, 16)
        for c in range(8):
            P.inc(sy.dma_start(out=winf[:, c * 1024:(c + 1) * 1024],
                               in_=dd["WinT_d"][c * 128:(c + 1) * 128, :]), ld, 16)
        P.inc(sy.dma_start(out=bcsb[:], in_=dd["rbfar_d"][0:1, :].to_broadcast([128, 32])), ld, 16)
    with p.use("vector") as v:
        P.wait(v, ld)
        P.inc(v.memset(shiftc[:], -20.0), p.sem("bc"), 1)
        P.inc(v.tensor_scalar(out=bcsb[:], in0=bcsb[:], scalar1=-20.0, scalar2=None,
                              op0=ALU.add), p.sem("bc"), 1)
    with p.use("tensor") as t:
        P.wait(t, ld)
    emit_ln_xnT(p, S, x_d, dd["xnTd_d"], xnTbf, mu_all, rstd_all, idfsb, "l0")
    barrier(p, "ln")
    emit_trevE(p, dd["trev_d"], dd["trevE_d"], NCOL_)
    r = emit_ffn_units(p, S, dd["xnTd_d"], winf, dd["kaT_d"], dd["kbT_d"], uf, dd["ev_d"],
                       iota16, gall, merged)
    emit_ffn_tail(p, S, gall, merged, dd["WoutT_d"], dd["lnw_d"][0:1, :], dd["lnb_d"][0:1, :],
                  dd["g_d"], idsb, r["s_wv"])
    barrier(p, "preattn")
    es_a.close()
    p._ctab = None
    emit_attention(p, S, "a0", xnTbf, dd["WqT_d"], dd["WkT_d"], dd["WvT_d"], dd["WoT_d"],
                   bqs, bks, bvs, dd["g_d"], x_d, dd["o_x1"], dd["bo_d"][0:1, :], dd["z_d"],
                   causal=False, trevE_dram=dd["trevE_d"], bconst_sb=bcsb, CBAND=CB_)
    return xnTbf


def build_l1(S):
    nc = bass.Bass("TRN2", target_bir_lowering=False)
    dr = lambda n, s, dt=F32, k="ExternalInput": nc.dram_tensor(n, s, dt, kind=k).ap()
    x_d = dr("x1", [S, D])
    Wk1T_d = dr("Wk1T", [D, D]); Wq1T_d = dr("Wq1T", [D, D]); Wv1T_d = dr("Wv1T", [D, D])
    Wp1T_d = dr("Wp1T", [D, D]); bp1_d = dr("bp1", [1, D])
    wg_d = dr("wg", [D, 16]); bg_d = dr("bg", [1, 16])
    W1cat_d = dr("W1cat", [1025, 4096]); W2cat_d = dr("W2cat", [4096, 1024])
    b2_d = dr("b2", [16, 1024]); gam_d = dr("gam", [1, D]); bet_d = dr("bet", [1, D])
    idf_d = dr("idf", [128, 128]); ctab_d = dr("ctab", [128, 1280], BF16)
    W1bf_d = dr("W1bf", [1025, 4096], BF16, k="Internal")
    W2bf_d = dr("W2bf", [4096, 1024], BF16, k="Internal")
    selrow_d = dr("selrow", [16, S], F32, k="Internal")
    xnTd_d = dr("xnTd1", [D, S], F32, k="Internal")
    g_d = dr("g_d1", [S, D], F32, k="Internal")
    z_d = dr("z_d1", [1, D], F32, k="Internal")
    o_x2 = dr("out", [S, D], F32, k="ExternalOutput")
    es = ExitStack()
    with es:
        blk = es.enter_context(nc.Block())
        p = P(nc, es, blk)
        p._cast_buf = [sbuf(p, "castbuf%d" % i, [128, 1024]) for i in range(2)]
        p._cast_w = 0
        emit_l1(p, S, locals())
    return nc


def emit_l1(p, S, dd):
    NT = S // 128
    x_d = dd["x_d"]
    xnTbf = dd.get("xnTbf_shared")
    if xnTbf is None:
        xnTbf = [sbuf(p, "xnTb%d" % j, [128, S], BF16) for j in range(8)]
    idfsb = dd.get("idfsb_shared")
    newconsts = idfsb is None
    es_b = ExitStack()
    if newconsts:
        idfsb = sbuf(p, "idfsb", [128, 128])
        shiftc = sbuf(p, "shiftc", [128, 1])
        p._shiftcol = shiftc
        mu_all = sbuf(p, "mu", [128, NT])
        rstd_all = sbuf(p, "rstd", [128, NT])
    else:
        shiftc = p._shiftcol
        mu_all = dd["mu_shared"]
        rstd_all = dd["rstd_shared"]
    ctab = sbuf(p, "ctabs", [128, 1280], BF16)
    selT_bf = sbuf(p, "selTbf", [16, S], BF16)
    bvz = sbuf(p, "bvz", [128, 8])
    p._ctab = ctab
    ld1 = p.sem("ld1")
    with p.use("sync") as sy:
        if newconsts:
            P.inc(sy.dma_start(out=idfsb[:], in_=dd["idf_d"][:]), ld1, 16)
        P.inc(sy.dma_start(out=ctab[:], in_=dd["ctab_d"][:]), ld1, 16)
    with p.use("vector") as v:
        P.wait(v, ld1)
        if newconsts:
            P.inc(v.memset(shiftc[:], -20.0), p.sem("ms1"), 1)
        P.inc(v.memset(bvz[:], 0.0), p.sem("ms1"), 1)
    emit_l1_cast_w(p, dd["W1cat_d"], dd["W2cat_d"], dd["W1bf_d"], dd["W2bf_d"])
    with p.use("tensor") as t:
        P.wait(t, ld1)
    emit_ln_xnT(p, S, x_d, dd["xnTd_d"], xnTbf, mu_all, rstd_all, idfsb, "l1")
    barrier(p, "ln1")
    emit_l1_gate(p, S, dd["xnTd_d"], dd["wg_d"], dd["bg_d"], dd["selrow_d"], selT_bf, idfsb)
    barrier(p, "gate1")
    emit_l1_experts(p, S, xnTbf, dd["W1bf_d"], dd["W2bf_d"], dd["b2_d"], dd["selrow_d"],
                    selT_bf, dd["gam_d"][0:1, :], dd["bet_d"][0:1, :], mu_all, rstd_all,
                    x_d, dd["g_d"], None)
    barrier(p, "ffn1")
    emit_attention(p, S, "a1", xnTbf, dd["Wq1T_d"], dd["Wk1T_d"], dd["Wv1T_d"], dd["Wp1T_d"],
                   None, None, bvz, dd["g_d"], x_d, dd["o_x2"], dd["bp1_d"][0:1, :],
                   dd["z_d"], causal=True)


def host_stage_full(inp):
    import ml_dtypes
    f32 = lambda a: np.ascontiguousarray(np.asarray(a), dtype=np.float32)
    c = {}
    c["WqT"] = f32(np.asarray(inp["L0_attn_Wq"]).T)
    c["WkT"] = f32(np.asarray(inp["L0_attn_Wk"]).T)
    c["WvT"] = f32(np.asarray(inp["L0_attn_Wv"]).T)
    c["WoT"] = f32(np.asarray(inp["L0_attn_Wo"]).T)
    c["bq"] = f32(np.asarray(inp["L0_attn_bq"]).reshape(8, 128).T)
    c["bk"] = f32(np.asarray(inp["L0_attn_bk"]).reshape(8, 128).T)
    c["bv"] = f32(np.asarray(inp["L0_attn_bv"]).reshape(8, 128).T)
    c["bo"] = f32(np.asarray(inp["L0_attn_bo"]).reshape(1, 1024))
    rb = f32(inp["L0_attn_rel_bias"])
    c["rbfar"] = f32(np.stack([rb[:, 0], rb[:, 256]], 1).reshape(1, 32))
    rr = np.arange(128)[:, None]
    dmat = np.arange(NCOL_)[None, :] - rr - CB_
    idx = np.clip(dmat, -128, 128) + 128
    c["trev"] = f32(rb[:, idx])
    c["WinT"] = f32(np.asarray(inp["L0_ffn_W_in"]).T)
    c["kaT"] = f32(np.transpose(np.asarray(inp["L0_ffn_keys_a"]), (0, 2, 1)))
    c["kbT"] = f32(np.transpose(np.asarray(inp["L0_ffn_keys_b"]), (0, 2, 1)))
    c["ush"] = f32(np.asarray(inp["L0_ffn_u_shared"]).reshape(128, 1))
    c["ev"] = f32(inp["L0_ffn_expert_v"])
    c["WoutT"] = f32(np.asarray(inp["L0_ffn_W_out"]).T)
    c["lnw"] = f32(np.asarray(inp["L0_ffn_ln_w"]).reshape(1, 1024))
    c["lnb"] = f32(np.asarray(inp["L0_ffn_ln_b"]).reshape(1, 1024))
    wk3 = np.asarray(inp["L1_attn_w_kqv"]).reshape(16 * 64, 3, D)
    c["Wk1T"] = f32(wk3[:, 0, :].T)
    c["Wq1T"] = f32(wk3[:, 1, :].T)
    c["Wv1T"] = f32(wk3[:, 2, :].T)
    c["Wp1T"] = f32(np.asarray(inp["L1_attn_w_proj"]).T)
    c["bp1"] = f32(np.asarray(inp["L1_attn_b_proj"]).reshape(1, 1024))
    c["wg"] = f32(inp["L1_ffn_wg"])
    c["bg"] = f32(np.asarray(inp["L1_ffn_bg"]).reshape(1, 16))
    w1 = np.asarray(inp["L1_ffn_w1"])
    W1 = np.transpose(w1, (1, 0, 2)).reshape(D, 16 * 256)
    c["W1cat"] = f32(np.concatenate([W1, np.asarray(inp["L1_ffn_b1"]).reshape(1, 16 * 256)], 0))
    c["W2cat"] = f32(np.asarray(inp["L1_ffn_w2"]).reshape(16 * 256, D))
    c["b2"] = f32(inp["L1_ffn_b2"])
    c["gam"] = f32(np.asarray(inp["L1_ffn_gamma"]).reshape(1, 1024))
    c["bet"] = f32(np.asarray(inp["L1_ffn_beta"]).reshape(1, 1024))
    c["idbf"] = np.eye(128, dtype=ml_dtypes.bfloat16)
    c["idf"] = np.eye(128, dtype=np.float32)
    c["iota16"] = np.broadcast_to(np.arange(16, dtype=np.float32), (128, 16)).copy()
    cm = (np.arange(1280)[None, :] - 384 - np.arange(128)[:, None]) >= 0
    c["ctab"] = cm.astype(ml_dtypes.bfloat16)
    return c


_L0_KEYS = ["WinT", "kaT", "kbT", "ush", "ev", "WoutT", "lnw", "lnb", "WqT", "WkT",
            "WvT", "WoT", "bq", "bk", "bv", "bo", "trev", "rbfar", "idbf", "idf", "iota16"]
_L1_KEYS = ["Wk1T", "Wq1T", "Wv1T", "Wp1T", "bp1", "wg", "bg", "W1cat", "W2cat", "b2",
            "gam", "bet", "idf", "ctab"]


def kernel(**inputs):
    S = SEQ
    c = host_stage_full(inputs)
    x_full = np.ascontiguousarray(np.asarray(inputs["L0_attn_x"], dtype=np.float32))
    ncores = 8
    nc0 = build_l0(S)
    maps0 = []
    for core in range(ncores):
        m = {k: c[k] for k in _L0_KEYS}
        m["x0"] = np.ascontiguousarray(x_full[core])
        maps0.append(m)
    res0 = run_bass_kernel_spmd(nc0, maps0, core_ids=list(range(ncores)))
    x1s = [res0.results[i]["x1"] for i in range(ncores)]
    nc1 = build_l1(S)
    maps1 = []
    for core in range(ncores):
        m = {k: c[k] for k in _L1_KEYS}
        m["x1"] = np.ascontiguousarray(x1s[core])
        maps1.append(m)
    res1 = run_bass_kernel_spmd(nc1, maps1, core_ids=list(range(ncores)))
    out = np.stack([res1.results[i]["out"] for i in range(ncores)], 0)
    return out
